# version 01
# baseline (speedup 1.0000x reference)
import numpy as np
import ml_dtypes
from contextlib import ExitStack

import concourse.bass as bass
import concourse.mybir as mybir
from concourse.bass_utils import run_bass_kernel_spmd

F32 = mybir.dt.float32
BF16 = mybir.dt.bfloat16
U8 = mybir.dt.uint8
AF = mybir.ActivationFunctionType
ALU = mybir.AluOpType

B, T, DM, DI, N, K = 4, 4096, 1024, 2048, 16, 4
NDT = 8
TCA = 512
TCS = 1024
EPS = 1e-5
VC1 = float(2.0 ** 20 + 1)
MAGIC = float(3 * 2 ** 13)


class Ctx:
    ENGS = ("sync", "vector", "scalar", "tensor", "gpsimd")

    def __init__(self, nc, es):
        self.nc = nc
        self._es = es
        self.ops = {e: [] for e in self.ENGS}
        self.sems = {}
        self.cnt = {}
        for s in ("LD", "TE", "VE", "SE", "SY"):
            self.sem(s)

    def sem(self, name):
        if name not in self.sems:
            self.sems[name] = self._es.enter_context(self.nc.semaphore(name))
            self.cnt[name] = 0
        return self.sems[name]

    def em(self, eng, fn):
        self.ops[eng].append(fn)

    def wait(self, eng, sem, val=None):
        v = self.cnt[sem] if val is None else val
        if v > 0:
            self.em(eng, lambda e, s=self.sems[sem], v=v: e.wait_ge(s, v))
        return v

    def dma(self, eng, out, in_, sem="LD", **kw):
        self.sem(sem)
        self.cnt[sem] += 16
        s = self.sems[sem]
        self.em(eng, lambda e, o=out, i=in_, kw=kw, s=s:
                e.dma_start(out=o, in_=i, **kw).then_inc(s, 16))
        return self.cnt[sem]

    def op(self, eng, meth, *a, sem=None, **kw):
        if meth != "nop":
            self.em(eng, lambda e, a=a, kw=kw: getattr(e, meth)(*a, **kw))
        if eng in ("vector", "scalar") or (eng == "tensor" and sem is not None):
            if sem is not None:
                self.sem(sem)
                self.cnt[sem] += 1
                s = self.sems[sem]
                self.em(eng, lambda e, s=s: e.drain().then_inc(s, 1))
                return self.cnt[sem]
            self.em(eng, lambda e: e.drain())
            return None
        if sem is not None:
            self.sem(sem)
            self.cnt[sem] += 1
            s = self.sems[sem]
            self.em(eng, lambda e, s=s: e.nop().then_inc(s, 1))
            return self.cnt[sem]
        return None

    def barrier(self):
        self.cnt["SY"] += 5
        tgt = self.cnt["SY"]
        for eng in self.ENGS:
            if eng in ("vector", "scalar", "tensor", "gpsimd"):
                self.em(eng, lambda e: e.drain())
            self.em(eng, lambda e, s=self.sems["SY"]: e.sem_inc(s, 1))
        for eng in self.ENGS:
            self.em(eng, lambda e, s=self.sems["SY"], v=tgt: e.wait_ge(s, v))

    def replay(self, blk):
        for eng in self.ENGS:
            ops = self.ops[eng]
            getattr(blk, eng)(lambda e, ops=ops: [f(e) for f in ops])


def _ap(t, off, *dims):
    return bass.AP(t.tensor if isinstance(t, bass.AP) else t, off, [list(d) for d in dims])



def build_l1(debug=()):
    nc = bass.Bass("TRN2", target_bir_lowering=False, debug=False)
    xb = nc.dram_tensor("xb", [T, DM], F32, kind="ExternalInput").ap()
    w_in_s = nc.dram_tensor("w_in_s", [24 * 128, DM], F32, kind="ExternalInput").ap()
    b_in_s = nc.dram_tensor("b_in_s", [128, 24], F32, kind="ExternalInput").ap()
    w_conv_s = nc.dram_tensor("w_conv_s", [128, 16 * K], F32, kind="ExternalInput").ap()
    b_conv_s = nc.dram_tensor("b_conv_s", [128, 16], F32, kind="ExternalInput").ap()
    w_x_s = nc.dram_tensor("w_x_s", [9 * 128, DI], F32, kind="ExternalInput").ap()
    b_x_s = nc.dram_tensor("b_x_s", [128, 9], F32, kind="ExternalInput").ap()
    a_log_s = nc.dram_tensor("a_log_s", [128, NDT * N], F32, kind="ExternalInput").ap()
    d_s = nc.dram_tensor("d_s", [128, NDT], F32, kind="ExternalInput").ap()
    eye_in = nc.dram_tensor("eye_in", [128, 128], BF16, kind="ExternalInput").ap()
    yg = nc.dram_tensor("yg", [128, NDT, T], BF16, kind="ExternalOutput").ap()
    dbg_t = {}
    for name, shape, dt in debug:
        dbg_t[name] = nc.dram_tensor("dbg_" + name, shape, dt, kind="ExternalOutput").ap()
    xi_h = nc.dram_tensor("xi_h", [128, NDT, T], BF16).ap()
    dl_h = nc.dram_tensor("dl_h", [128, NDT, T], F32).ap()
    rs_h = nc.dram_tensor("rs_h", [128, NDT, T], F32).ap()
    bc_h = nc.dram_tensor("bc_h", [32, T], BF16).ap()

    es = ExitStack()
    with es:
        c = Ctx(nc, es)
        sb = lambda n, s, d=F32: es.enter_context(nc.sbuf_tensor(n, s, d))
        eye = sb("eye", [128, 128], BF16)
        b_in_t = sb("b_in_t", [128, 24])
        b_conv_t = sb("b_conv_t", [128, 16])
        b_x_t = sb("b_x_t", [128, 9])
        wconv_t = sb("wconv_t", [128, 16 * K])
        a_t = sb("a_t", [128, NDT * N])
        d_t = sb("d_t", [128, NDT])
        ceps = sb("ceps", [128, 1])
        states = sb("states", [128, NDT * N])
        es_ab = es.enter_context(ExitStack())
        sba = lambda n, s, d=F32: es_ab.enter_context(nc.sbuf_tensor(n, s, d))
        winT = sba("winT", [128, 8, 24 * 128], BF16)
        wxT = sba("wxT", [128, 16, 9 * 128], BF16)
        diag = sba("diag", [128, 16 * K, 128], BF16)

        c.op("vector", "memset", ceps[:], EPS)
        c.op("vector", "memset", states[:], 0.0)
        for dst, src in ((b_in_t, b_in_s), (b_conv_t, b_conv_s), (b_x_t, b_x_s),
                         (wconv_t, w_conv_s), (a_t, a_log_s), (d_t, d_s), (eye, eye_in)):
            c.dma("sync", dst[:], src)
        ld0 = c.cnt["LD"]
        c.wait("scalar", "LD", ld0)
        c.op("scalar", "activation", a_t[:], a_t[:], AF.Exp, sem="SE")
        c.wait("vector", "SE")
        c.op("vector", "tensor_scalar_mul", a_t[:], a_t[:], -1.0)
        c.wait("vector", "LD", ld0)
        for i in range(16 * K):
            c.op("vector", "tensor_scalar_mul", diag[:, i, :], eye[:], wconv_t[:, i:i + 1])
        c.op("vector", "nop", sem="VE")

        es_stg = ExitStack()
        stg_f = es_stg.enter_context(nc.sbuf_tensor("stg_f", [128, 2, DI], F32))
        stg_b = es_stg.enter_context(nc.sbuf_tensor("stg_b", [128, 2, DI], BF16))

        tr_done = [0, 0]

        def prep_w(src, n_ct, kdim, dstT):
            for ct in range(n_ct):
                p = ct % 2
                if tr_done[p]:
                    c.wait("sync", "WT%d" % p, tr_done[p])
                ld = c.dma("sync", stg_f[:, p, 0:kdim], src[ct * 128:(ct + 1) * 128, :],
                           sem="WL%d" % p)
                c.wait("vector", "WL%d" % p, ld)
                c.wait("vector", "WT%d" % p, tr_done[p])
                cd = c.op("vector", "tensor_copy", stg_b[:, p, 0:kdim],
                          stg_f[:, p, 0:kdim], sem="VE")
                c.wait("sync", "VE", cd)
                tr_done[p] = c.dma("sync", dstT[:, :, ct * 128:(ct + 1) * 128],
                                   stg_b[:, p, 0:kdim], transpose=True, sem="WT%d" % p)
        prep_w(w_in_s, 24, DM, winT)
        prep_w(w_x_s, 9, DI, wxT)
        c.wait("sync", "WT0")
        c.wait("sync", "WT1")
        c.wait("sync", "LD")
        c.barrier()
        es_stg.close()

        xtok = sba("xtok", [128, 1, 4, DM])
        scr = sba("scr", [128, DM])
        ssum = sba("ssum", [128, 1, 4])
        ssq = sba("ssq", [128, 1, 4])
        stat = sba("stat", [128, 1, 12])
        vtmp = sba("vtmp", [128, 4])
        xntok = sba("xntok", [128, 1, 4, DM], BF16)
        xnT = sba("xnT", [128, 1, 8, TCA], BF16)
        xipre = sba("xipre", [128, 16, 3 + TCA], BF16)
        xichk = sba("xichk", [128, 1, 16, TCA], BF16)
        reschk = sba("reschk", [128, 1, NDT, TCA])
        dlchk = sba("dlchk", [128, 1, NDT, TCA])
        bcchk = sba("bcchk", [32, 1, TCA], BF16)
        sps = es_ab.enter_context(nc.psum_tensor("sps", [128, 2, TCA], F32))
        cps = es_ab.enter_context(nc.psum_tensor("cps", [128, 2, TCA], F32))
        pps = es_ab.enter_context(nc.psum_tensor("pps", [128, 2, TCA], F32))

        c.op("vector", "memset", xipre[:, :, 0:3], 0.0)
        c.op("vector", "nop", sem="VE")

        NCH = T // TCA
        xn_ve = [0, 0]
        xr_mm = [0, 0]
        ev_ve = {}
        ev_se = {}
        conv_mm = 0
        xi_st = [0, 0]
        dl_st = [0, 0]
        rs_st = [0, 0]
        bc_st = [0, 0]
        for tc in range(NCH):
            pb = 0
            xin = _ap(xb, tc * TCA * DM, (DM, 128), (128 * DM, 4), (1, DM))
            c.wait("sync", "VE", xn_ve[pb])
            ldx = c.dma("sync", xtok[:, pb, :, :], xin, sem="XL")
            c.wait("scalar", "XL", ldx)
            for j in range(4):
                c.op("scalar", "activation", scr[:], xtok[:, pb, j, :], AF.Identity,
                     accum_out=ssum[:, pb, j:j + 1])
                c.op("scalar", "activation", scr[:], xtok[:, pb, j, :], AF.Square,
                     accum_out=ssq[:, pb, j:j + 1])
            st_se = c.op("scalar", "nop", sem="SE")
            c.wait("vector", "SE", st_se)
            c.op("vector", "tensor_scalar_mul", stat[:, pb, 0:4], ssum[:, pb, :], -1.0 / DM)
            c.op("vector", "tensor_scalar_mul", stat[:, pb, 4:8], ssq[:, pb, :], 1.0 / DM)
            c.op("vector", "tensor_tensor", vtmp[:], stat[:, pb, 0:4],
                 stat[:, pb, 0:4], ALU.mult)
            vw = c.op("vector", "tensor_sub", stat[:, pb, 4:8], stat[:, pb, 4:8],
                      vtmp[:], sem="VE")
            c.wait("scalar", "VE", vw)
            c.op("scalar", "activation", stat[:, pb, 4:8], stat[:, pb, 4:8],
                 AF.Ln, bias=ceps[:, 0:1])
            iv_se = c.op("scalar", "activation", stat[:, pb, 8:12], stat[:, pb, 4:8],
                         AF.Exp, scale=-0.5, sem="SE")
            c.wait("vector", "SE", iv_se)
            c.wait("vector", "XL", ldx)
            for j in range(4):
                c.op("vector", "tensor_scalar", xntok[:, pb, j, :], xtok[:, pb, j, :],
                     stat[:, pb, j:j + 1], stat[:, pb, 8 + j:9 + j], ALU.add, ALU.mult)
            xn_ve[pb] = c.op("vector", "nop", sem="VE")
            if "xn" in dbg_t and tc == 0:
                c.wait("gpsimd", "VE", xn_ve[pb])
                c.dma("gpsimd", dbg_t["xn"], xntok[:, 0, :, :])
            c.wait("sync", "VE", xn_ve[pb])
            c.wait("sync", "TE", xr_mm[pb])
            for j in range(4):
                trx = c.dma("sync", xnT[:, pb, :, j * 128:(j + 1) * 128],
                            xntok[:, pb, j, :], transpose=True, sem="XT")
            c.wait("tensor", "XT", trx)
            c.wait("tensor", "TE", conv_mm)
            for ct in range(24):
                pp = ct % 2
                c.wait("tensor", "VE", ev_ve.get(("xr", pp), 0))
                c.wait("tensor", "SE", ev_se.get(("xr", pp), 0))
                for kc in range(8):
                    c.op("tensor", "matmul", sps[:, pp, :],
                         winT[:, kc, ct * 128:(ct + 1) * 128],
                         xnT[:, pb, kc, :], start=(kc == 0), stop=(kc == 7))
                mm = c.op("tensor", "nop", sem="TE")
                if ct < 16:
                    c.wait("vector", "TE", mm)
                    ev_ve[("xr", pp)] = c.op(
                        "vector", "tensor_scalar_add", xipre[:, ct, 3:3 + TCA],
                        sps[:, pp, :], b_in_t[:, ct:ct + 1], sem="VE")
                else:
                    ct8 = ct - 16
                    c.wait("scalar", "TE", mm)
                    c.wait("scalar", "RS", rs_st[pb])
                    rs_st[pb] = 0
                    ev_se[("xr", pp)] = c.op(
                        "scalar", "activation", reschk[:, pb, ct8, :], sps[:, pp, :],
                        AF.Silu, bias=b_in_t[:, ct:ct + 1], sem="SE")
            xr_mm[pb] = c.cnt["TE"]
            c.wait("gpsimd", "SE", ev_se[("xr", 1)])
            c.wait("gpsimd", "SE", ev_se[("xr", 0)])
            rs_st[pb] = c.dma("gpsimd", rs_h[:, :, tc * TCA:(tc + 1) * TCA],
                              reschk[:, pb, :, :], sem="RS")
            if "xipre" in dbg_t and tc == 0:
                c.wait("gpsimd", "VE", ev_ve[("xr", 1)])
                c.wait("gpsimd", "VE", ev_ve[("xr", 0)])
                c.dma("gpsimd", dbg_t["xipre"], xipre[:, :, 3:3 + TCA])
            c.wait("tensor", "VE", ev_ve[("xr", 0)])
            c.wait("tensor", "VE", ev_ve[("xr", 1)])
            c.wait("scalar", "XS", xi_st[pb])
            for dt in range(16):
                pp = dt % 2
                c.wait("tensor", "SE", ev_se.get(("cv", pp), 0))
                for k in range(K):
                    c.op("tensor", "matmul", cps[:, pp, :], diag[:, dt * K + k, :],
                         xipre[:, dt, k:k + TCA], start=(k == 0), stop=(k == K - 1))
                mm = c.op("tensor", "nop", sem="TE")
                c.wait("scalar", "TE", mm)
                ev_se[("cv", pp)] = c.op(
                    "scalar", "activation", xichk[:, pb, dt, :], cps[:, pp, :],
                    AF.Silu, bias=b_conv_t[:, dt:dt + 1], sem="SE")
            conv_mm = c.cnt["TE"]
            c.wait("vector", "TE", conv_mm)
            c.op("vector", "tensor_copy", xipre[:, :, 0:3],
                 xipre[:, :, TCA:TCA + 3], sem="VE")
            c.wait("gpsimd", "SE", ev_se[("cv", 0)])
            c.wait("gpsimd", "SE", ev_se[("cv", 1)])
            xi_st[pb] = c.dma("gpsimd", xi_h[:, :, tc * TCA:(tc + 1) * TCA],
                              xichk[:, pb, 0:NDT, :], sem="XS")
            c.wait("tensor", "SE", ev_se[("cv", 0)])
            c.wait("tensor", "SE", ev_se[("cv", 1)])
            for ct in range(9):
                pp = ct % 2
                c.wait("tensor", "SE", ev_se.get(("pj", pp), 0))
                c.wait("tensor", "VE", ev_ve.get(("pj", pp), 0))
                for kc in range(16):
                    c.op("tensor", "matmul", pps[:, pp, :],
                         wxT[:, kc, ct * 128:(ct + 1) * 128],
                         xichk[:, pb, kc, :], start=(kc == 0), stop=(kc == 15))
                mm = c.op("tensor", "nop", sem="TE")
                if ct < 8:
                    c.wait("scalar", "TE", mm)
                    c.wait("scalar", "DS", dl_st[pb])
                    dl_st[pb] = 0
                    c.op("scalar", "activation", scr[:, 0:TCA], pps[:, pp, :],
                         AF.Exp, bias=b_x_t[:, ct:ct + 1])
                    ev_se[("pj", pp)] = c.op(
                        "scalar", "activation", dlchk[:, pb, ct, :], scr[:, 0:TCA],
                        AF.Ln, bias=1.0, sem="SE")
                else:
                    c.wait("vector", "TE", mm)
                    c.wait("vector", "BS", bc_st[pb])
                    bc_st[pb] = 0
                    ev_ve[("pj", pp)] = c.op(
                        "vector", "tensor_scalar_add", bcchk[:, pb, :],
                        pps[0:32, pp, :], b_x_t[0:32, ct:ct + 1], sem="VE")
            c.wait("gpsimd", "SE", ev_se[("pj", 0)])
            c.wait("gpsimd", "SE", ev_se[("pj", 1)])
            dl_st[pb] = c.dma("gpsimd", dl_h[:, :, tc * TCA:(tc + 1) * TCA],
                              dlchk[:, pb, :, :], sem="DS")
            c.wait("gpsimd", "VE", ev_ve[("pj", 0)])
            bc_st[pb] = c.dma("gpsimd", bc_h[:, tc * TCA:(tc + 1) * TCA],
                              bcchk[:, pb, :], sem="BS")
        for sm in ("RS", "DS", "BS", "XS"):
            c.wait("gpsimd", sm)
        c.barrier()
        es_ab.close()

        bcB = sb("bcB", [128, N, TCS], BF16)
        bcC = sb("bcC", [128, N, TCS], BF16)
        dls = sb("dls", [128, 2, TCS])
        xis = sb("xis", [128, 2, TCS], BF16)
        rss = sb("rss", [128, 2, TCS])
        wt = sb("wt", [128, 2, TCS], BF16)
        ut = sb("ut", [128, 2, TCS], BF16)
        dat = sb("dat", [128, 2, TCS])
        dbt = sb("dbt", [128, 2, TCS], BF16)
        ht = sb("ht", [128, 2, TCS], BF16)
        tnc = sb("tnc", [128, 4, TCS], BF16)
        g1 = sb("g1", [128, 2, TCS])
        ygt = sb("ygt", [128, 2, TCS], BF16)
        yps = es.enter_context(nc.psum_tensor("yps", [128, 2, TCS], F32))

        NSC = T // TCS
        last_cons = [0, 0]
        last_gate = [0, 0]
        last_ygdma = [0, 0]
        last_ymm = [0, 0, 0, 0]
        last_tnc = [0, 0]
        prev_chunk_ymm = 0
        for tc in range(NSC):
            c.wait("sync", "TE", prev_chunk_ymm)
            c.wait("sync", "BS")
            bcl = 0
            for n in range(N):
                c.dma("sync", bcB[:, n, :],
                      _ap(bc_h, n * T + tc * TCS, (0, 128), (1, TCS)), sem="BCL")
                bcl = c.dma("sync", bcC[:, n, :],
                            _ap(bc_h, (16 + n) * T + tc * TCS, (0, 128), (1, TCS)), sem="BCL")
            for dt in range(NDT):
                pd = dt % 2
                sl = slice(tc * TCS, (tc + 1) * TCS)
                c.wait("sync", "VE", last_cons[pd])
                c.dma("sync", dls[:, pd, :], dl_h[:, dt, sl], sem="SL%d" % pd)
                c.dma("sync", xis[:, pd, :], xi_h[:, dt, sl], sem="SL%d" % pd)
                sld = c.dma("sync", rss[:, pd, :], rs_h[:, dt, sl], sem="SL%d" % pd)
                c.wait("vector", "SL%d" % pd, sld)
                c.op("vector", "tensor_mul", wt[:, pd, :], dls[:, pd, :], xis[:, pd, :])
                c.op("vector", "tensor_scalar_mul", ut[:, pd, :], xis[:, pd, :],
                     d_t[:, dt:dt + 1], sem="VE")
                c.wait("scalar", "SL%d" % pd, sld)
                for n in range(N):
                    pn = n % 2
                    idx = dt * N + n
                    c.wait("scalar", "VE", last_tnc[pn])
                    da_se = c.op("scalar", "activation", dat[:, pn, :], dls[:, pd, :],
                                 AF.Exp, scale=a_t[:, idx:idx + 1], sem="SE")
                    c.wait("vector", "BCL", bcl)
                    c.wait("vector", "SE", da_se)
                    c.wait("vector", "TE", last_ymm[n % 4])
                    c.op("vector", "tensor_mul", dbt[:, pn, :], wt[:, pd, :], bcB[:, n, :])
                    c.op("vector", "tensor_tensor_scan", ht[:, pn, :], dat[:, pn, :],
                         dbt[:, pn, :], states[:, idx:idx + 1], ALU.mult, ALU.add)
                    c.op("vector", "tensor_copy", states[:, idx:idx + 1],
                         ht[:, pn, TCS - 1:TCS])
                    last_tnc[pn] = c.op("vector", "tensor_mul", tnc[:, n % 4, :],
                                        ht[:, pn, :], bcC[:, n, :], sem="VE")
                    c.wait("tensor", "VE", last_tnc[pn])
                    if n == 0:
                        c.wait("tensor", "VE", last_gate[pd])
                    for h2 in range(TCS // TCA):
                        c.op("tensor", "matmul", yps[:, pd, h2 * TCA:(h2 + 1) * TCA],
                             eye[:], tnc[:, n % 4, h2 * TCA:(h2 + 1) * TCA],
                             start=(n == 0), stop=(n == N - 1))
                    last_ymm[n % 4] = c.op("tensor", "nop", sem="TE")
                c.wait("vector", "TE", last_ymm[(N - 1) % 4])
                c.wait("vector", "YS%d" % pd, last_ygdma[pd])
                c.op("vector", "scalar_tensor_tensor", g1[:, pd, :], yps[:, pd, :],
                     1.0, ut[:, pd, :], ALU.mult, ALU.add)
                gv = c.op("vector", "tensor_mul", ygt[:, pd, :], g1[:, pd, :],
                          rss[:, pd, :], sem="VE")
                last_gate[pd] = gv
                last_cons[pd] = gv
                c.wait("gpsimd", "VE", gv)
                last_ygdma[pd] = c.dma("gpsimd", yg[:, dt, sl], ygt[:, pd, :],
                                       sem="YS%d" % pd)
            prev_chunk_ymm = c.cnt["TE"]
        c.wait("gpsimd", "YS0")
        c.wait("gpsimd", "YS1")
        c.replay(es.enter_context(nc.Block()))
    return nc



def build_l2(debug=()):
    TT2 = 2048
    NTT = TT2 // 128
    nc = bass.Bass("TRN2", target_bir_lowering=False, debug=False)
    ygb = nc.dram_tensor("ygb", [128, 16, TT2], BF16, kind="ExternalInput").ap()
    xb_t = nc.dram_tensor("xb_t", [TT2, DM], F32, kind="ExternalInput").ap()
    w_out = nc.dram_tensor("w_out", [DM, DI], F32, kind="ExternalInput").ap()
    b_out_r = nc.dram_tensor("b_out_r", [1, DM], F32, kind="ExternalInput").ap()
    kern = nc.dram_tensor("kern", [128, 8, DM], BF16, kind="ExternalInput").ap()
    s_in = nc.dram_tensor("s_in", [1, 1], F32, kind="ExternalInput").ap()
    s_k = nc.dram_tensor("s_k", [1, 1], F32, kind="ExternalInput").ap()
    fbias = nc.dram_tensor("fbias", [1, DM], BF16, kind="ExternalInput").ap()
    out_t = nc.dram_tensor("out_t", [TT2, DM], F32, kind="ExternalOutput").ap()
    dbg_t = {}
    for name, shape, dt in debug:
        dbg_t[name] = nc.dram_tensor("dbg_" + name, shape, dt, kind="ExternalOutput").ap()

    es = ExitStack()
    with es:
        c = Ctx(nc, es)
        sb = lambda n, s, d=F32: es.enter_context(nc.sbuf_tensor(n, s, d))

        ygs = sb("ygs", [128, 16, TT2], BF16)
        woT = sb("woT", [128, 16, DM], BF16)
        kq = sb("kq", [128, 8, DM], BF16)
        bout_bc = sb("bout_bc", [128, DM])
        fb_bc = sb("fb_bc", [128, DM], BF16)
        sinv = sb("sinv", [128, 3])
        ceps = sb("ceps", [128, 1])
        scr = sb("scr", [128, DM])
        scrq = sb("scrq", [128, 4, DM])
        msk = sb("msk", [128, DM], U8)
        qf = sb("qf", [128, DM])
        reck = sb("reck", [128, 1])
        reck2 = sb("reck2", [128, 1])
        ksrc = sb("ksrc", [128, DM])
        es_set = es.enter_context(ExitStack())
        sbs = lambda n, s2, d=F32: es_set.enter_context(nc.sbuf_tensor(n, s2, d))

        c.op("vector", "memset", ceps[:], EPS)
        kbf = sbs("kbf", [128, 8, DM], BF16)
        c.dma("sync", ygs[:], ygb)
        c.dma("sync", bout_bc[:], _ap(b_out_r, 0, (0, 128), (1, DM)))
        c.dma("sync", fb_bc[:], _ap(fbias, 0, (0, 128), (1, DM)))
        c.dma("sync", sinv[:, 0:1], _ap(s_in, 0, (0, 128), (1, 1)))
        c.dma("sync", sinv[:, 1:2], _ap(s_k, 0, (0, 128), (1, 1)))
        c.dma("sync", reck[:], _ap(s_k, 0, (0, 128), (1, 1)))
        c.dma("sync", kbf[:], kern)
        ld_all = c.cnt["LD"]
        stg_f = sbs("stg2_f", [128, 2, DI])
        stg_b = sbs("stg2_b", [128, 2, DI], BF16)
        tr_done = [0, 0]
        for ct in range(8):
            p = ct % 2
            if tr_done[p]:
                c.wait("sync", "WT%d" % p, tr_done[p])
            ld = c.dma("sync", stg_f[:, p, :], w_out[ct * 128:(ct + 1) * 128, :],
                       sem="WL%d" % p)
            c.wait("vector", "WL%d" % p, ld)
            c.wait("vector", "WT%d" % p, tr_done[p])
            cd = c.op("vector", "tensor_copy", stg_b[:, p, :], stg_f[:, p, :], sem="VE")
            c.wait("sync", "VE", cd)
            tr_done[p] = c.dma("sync", woT[:, :, ct * 128:(ct + 1) * 128],
                               stg_b[:, p, :], transpose=True, sem="WT%d" % p)
        c.wait("vector", "LD", ld_all)
        c.op("vector", "tensor_mul", sinv[:, 1:2], sinv[:, 0:1], sinv[:, 1:2])
        c.op("vector", "reciprocal", sinv[:, 2:3], sinv[:, 0:1], sem="VE")
        c.op("vector", "reciprocal", reck2[:], reck[:], sem="VE")

        def quant_tile(dst_bf, src_f32, width):
            t1, t2, hi = scrq[:, 0, 0:width], scrq[:, 1, 0:width], scrq[:, 2, 0:width]
            sq = scrq[:, 3, 0:width]
            c.op("vector", "tensor_scalar_mul", t1, src_f32, VC1)
            c.op("vector", "tensor_sub", t2, t1, src_f32)
            c.op("vector", "tensor_sub", hi, t1, t2)
            c.op("vector", "tensor_scalar", qf[:, 0:width], src_f32, MAGIC, MAGIC,
                 ALU.add, ALU.subtract)
            c.op("vector", "scalar_tensor_tensor", sq, src_f32, 1.0, src_f32,
                 ALU.mult, ALU.mult)
            c.op("vector", "tensor_single_scalar", msk[:, 0:width], sq,
                 float(2.0 ** -10), ALU.is_ge)
            c.op("vector", "copy_predicated", qf[:, 0:width], msk[:, 0:width], hi)
            c.op("vector", "tensor_scalar", dst_bf, qf[:, 0:width], 448.0, -448.0,
                 ALU.min, ALU.max)

        for kc in range(8):
            c.op("vector", "tensor_copy", scr[:], kbf[:, kc, :])
            c.op("vector", "tensor_scalar_mul", ksrc[:], scr[:], reck2[:, 0:1])
            quant_tile(kq[:, kc, :], ksrc[:], DM)
        c.op("vector", "nop", sem="VE")
        c.wait("sync", "LD", ld_all)
        c.wait("sync", "WT0")
        c.wait("sync", "WT1")
        c.barrier()
        es_set.close()

        xts = sb("xts", [128, 2, DM])
        x2t = sb("x2t", [128, 2, DM])
        x2n = sb("x2n", [128, 2, DM], BF16)
        x2nf = sb("x2nf", [128, DM])
        qb = sb("qb", [128, DM], BF16)
        x2qT = sb("x2qT", [128, 2, 8, 128], BF16)
        ffnb = sb("ffnb", [128, 2, DM], BF16)
        outt = sb("outt", [128, 2, DM])
        stt = sb("stt", [128, 8])
        aps = es.enter_context(nc.psum_tensor("aps", [128, 2, DM], F32))
        fps = es.enter_context(nc.psum_tensor("fps", [128, 2, DM], F32))

        prev = {}
        for tt in range(NTT):
            pb = tt % 2
            tsl = slice(tt * 128, (tt + 1) * 128)
            c.wait("sync", "VE", prev.get(("x2", tt - 2), 0))
            ldx = c.dma("sync", xts[:, pb, :],
                        _ap(xb_t, tt * 128 * DM, (DM, 128), (1, DM)), sem="XL%d" % pb)
            c.wait("tensor", "VE", prev.get(("aev", pb), 0))
            for kc in range(16):
                for hh in range(2):
                    c.op("tensor", "matmul", aps[:, pb, hh * 512:(hh + 1) * 512],
                         ygs[:, kc, tsl], woT[:, kc, hh * 512:(hh + 1) * 512],
                         start=(kc == 0), stop=(kc == 15))
            amm = c.op("tensor", "nop", sem="TE")
            c.wait("vector", "TE", amm)
            c.wait("vector", "XL%d" % pb, ldx)
            c.op("vector", "scalar_tensor_tensor", scr[:], aps[:, pb, :], 1.0,
                 bout_bc[:], ALU.mult, ALU.add)
            x2ve = c.op("vector", "tensor_add", x2t[:, pb, :], scr[:], xts[:, pb, :],
                        sem="VE")
            prev[("x2", tt)] = x2ve
            c.wait("scalar", "VE", x2ve)
            c.op("scalar", "activation", scr[:], x2t[:, pb, :], AF.Identity,
                 accum_out=stt[:, 0:1])
            sse = c.op("scalar", "activation", scr[:], x2t[:, pb, :], AF.Square,
                       accum_out=stt[:, 1:2], sem="SE")
            c.wait("vector", "SE", sse)
            c.op("vector", "tensor_scalar_mul", stt[:, 2:3], stt[:, 0:1], -1.0 / DM)
            c.op("vector", "tensor_scalar_mul", stt[:, 3:4], stt[:, 1:2], 1.0 / DM)
            c.op("vector", "tensor_tensor", stt[:, 4:5], stt[:, 2:3], stt[:, 2:3], ALU.mult)
            vve = c.op("vector", "tensor_sub", stt[:, 3:4], stt[:, 3:4], stt[:, 4:5],
                       sem="VE")
            c.wait("scalar", "VE", vve)
            c.op("scalar", "activation", stt[:, 3:4], stt[:, 3:4], AF.Ln, bias=ceps[:, 0:1])
            ivse = c.op("scalar", "activation", stt[:, 5:6], stt[:, 3:4], AF.Exp,
                        scale=-0.5, sem="SE")
            c.wait("vector", "SE", ivse)
            c.op("vector", "tensor_scalar", x2n[:, pb, :], x2t[:, pb, :],
                 stt[:, 2:3], stt[:, 5:6], ALU.add, ALU.mult)
            c.op("vector", "tensor_copy", x2nf[:], x2n[:, pb, :])
            c.op("vector", "tensor_scalar_mul", x2nf[:], x2nf[:], sinv[:, 2:3])
            quant_tile(qb[:], x2nf[:], DM)
            qve = c.op("vector", "nop", sem="VE")
            c.wait("sync", "VE", qve)
            c.wait("sync", "TE", prev.get(("fmm", pb), 0))
            trq = c.dma("sync", x2qT[:, pb, :, :], qb[:], transpose=True, sem="XT%d" % pb)
            c.wait("tensor", "XT%d" % pb, trq)
            c.wait("tensor", "VE", prev.get(("fev", pb), 0))
            for kc in range(8):
                for hh in range(2):
                    c.op("tensor", "matmul", fps[:, pb, hh * 512:(hh + 1) * 512],
                         x2qT[:, pb, kc, :], kq[:, kc, hh * 512:(hh + 1) * 512],
                         start=(kc == 0), stop=(kc == 7))
            fmm = c.op("tensor", "nop", sem="TE")
            prev[("fmm", pb)] = fmm
            c.wait("vector", "TE", fmm)
            c.op("vector", "scalar_tensor_tensor", ffnb[:, pb, :], fps[:, pb, :],
                 sinv[:, 1:2], fb_bc[:], ALU.mult, ALU.add)
            c.wait("vector", "OS%d" % pb, prev.get(("odma", tt - 2), 0))
            fev = c.op("vector", "tensor_add", outt[:, pb, :], x2t[:, pb, :],
                       ffnb[:, pb, :], sem="VE")
            prev[("fev", pb)] = fev
            prev[("aev", pb)] = fev
            c.wait("gpsimd", "VE", fev)
            c.wait("gpsimd", "OS%d" % pb, prev.get(("odma", tt - 2), 0))
            prev[("odma", tt)] = c.dma("gpsimd", out_t[tsl, :], outt[:, pb, :],
                                       sem="OS%d" % pb)
        c.wait("gpsimd", "OS0")
        c.wait("gpsimd", "OS1")
        c.replay(es.enter_context(nc.Block()))
    return nc



def _l1_inmaps(inputs):
    x = np.ascontiguousarray(inputs["L0_attn_x"])
    W_in = np.ascontiguousarray(inputs["L0_attn_W_in"])
    b_in = np.ascontiguousarray(inputs["L0_attn_b_in"])
    W_conv = np.ascontiguousarray(inputs["L0_attn_W_conv"])
    b_conv = np.ascontiguousarray(inputs["L0_attn_b_conv"])
    W_x = np.ascontiguousarray(inputs["L0_attn_W_x"])
    b_x = np.ascontiguousarray(inputs["L0_attn_b_x"])
    A_log = np.ascontiguousarray(inputs["L0_attn_A_log"])
    Dv = np.ascontiguousarray(inputs["L0_attn_D"])
    eye = np.eye(128, dtype=ml_dtypes.bfloat16)
    maps = []
    for cc in range(8):
        b, dh = cc // 2, cc % 2
        own = np.arange(1024 * dh, 1024 * dh + 1024)
        oth = np.arange(1024 * (1 - dh), 1024 * (1 - dh) + 1024)
        perm = np.concatenate([own, oth])
        w_in_s = np.concatenate([W_in[perm], W_in[DI + own]], axis=0)
        b_in_cat = np.concatenate([b_in[perm], b_in[DI + own]])
        b_in_s = np.ascontiguousarray(b_in_cat.reshape(24, 128).T)
        wc = W_conv[perm, 0, :]
        w_conv_s = np.ascontiguousarray(
            wc.reshape(16, 128, K).transpose(1, 0, 2).reshape(128, 16 * K))
        b_conv_s = np.ascontiguousarray(b_conv[perm].reshape(16, 128).T)
        wx_rows = np.concatenate([W_x[own], W_x[DI:DI + 32],
                                  np.zeros((96, DI), W_x.dtype)], axis=0)
        w_x_s = np.ascontiguousarray(wx_rows[:, perm])
        bx9 = np.zeros((128, 9), np.float32)
        bx9[:, 0:8] = b_x[own].reshape(8, 128).T
        bx9[0:32, 8] = b_x[DI:DI + 32]
        a_log_s = np.ascontiguousarray(
            A_log[own].reshape(8, 128, N).transpose(1, 0, 2).reshape(128, 8 * N))
        d_s = np.ascontiguousarray(Dv[own].reshape(8, 128).T)
        maps.append(dict(xb=x[b], w_in_s=w_in_s, b_in_s=b_in_s, w_conv_s=w_conv_s,
                         b_conv_s=b_conv_s, w_x_s=w_x_s, b_x_s=bx9,
                         a_log_s=a_log_s, d_s=d_s, eye_in=eye))
    return maps


def _l2_inmaps(inputs, yg_l1):
    x = np.ascontiguousarray(inputs["L0_attn_x"])
    W_out = np.ascontiguousarray(inputs["L0_attn_W_out"])
    b_out = np.ascontiguousarray(inputs["L0_attn_b_out"]).reshape(1, DM)
    kern = np.ascontiguousarray(inputs["L0_ffn_kernel"])
    s_in = np.ascontiguousarray(inputs["L0_ffn_input_scale"]).reshape(1, 1)
    s_k = np.ascontiguousarray(inputs["L0_ffn_kernel_scale"]).reshape(1, 1)
    fb = np.ascontiguousarray(inputs["L0_ffn_bias"]).reshape(1, DM)
    kern_r = np.ascontiguousarray(
        kern.reshape(8, 128, DM).transpose(1, 0, 2))
    maps = []
    for cc in range(8):
        b, th = cc // 2, cc % 2
        ygb = np.empty((128, 16, 2048), dtype=ml_dtypes.bfloat16)
        for g in range(16):
            src = yg_l1[2 * b + g // 8]
            ygb[:, g, :] = src[:, g % 8, th * 2048:(th + 1) * 2048]
        maps.append(dict(ygb=np.ascontiguousarray(ygb),
                         xb_t=np.ascontiguousarray(x[b, th * 2048:(th + 1) * 2048]),
                         w_out=W_out, b_out_r=b_out, kern=kern_r,
                         s_in=s_in, s_k=s_k, fbias=fb))
    return maps


def kernel(**inputs):
    nc1 = build_l1()
    r1 = run_bass_kernel_spmd(nc1, _l1_inmaps(inputs), core_ids=list(range(8)))
    yg_l1 = [r1.results[cc]["yg"] for cc in range(8)]
    nc2 = build_l2()
    r2 = run_bass_kernel_spmd(nc2, _l2_inmaps(inputs, yg_l1), core_ids=list(range(8)))
    out = np.empty((B, T, DM), np.float32)
    for cc in range(8):
        b, th = cc // 2, cc % 2
        out[b, th * 2048:(th + 1) * 2048] = r2.results[cc]["out_t"]
    return out
